# version 11
# baseline (speedup 1.0000x reference)
from collections import defaultdict

import numpy as np

import concourse.bass as bass
import concourse.mybir as mybir

F32 = mybir.dt.float32
BF16 = mybir.dt.bfloat16
FP8 = mybir.dt.float8e4
AF = mybir.ActivationFunctionType
ALU = mybir.AluOpType
AXX = mybir.AxisListType.X
DRMODE = mybir.MatmulPerfMode.DoubleRow

SCALE = float(np.float32(128.0) ** np.float32(0.5))


class Cfg:
    def __init__(self, D=2048, HID=8192, M=4096, HEADS=16, TOK=2048, HALVES=2):
        self.D, self.HID, self.M, self.HEADS, self.TOK, self.HALVES = D, HID, M, HEADS, TOK, HALVES
        self.TOKH = TOK // HALVES
        self.TT = self.TOKH // 128
        self.NC2 = self.TOKH // 512
        self.DT = D // 128
        self.MT = M // 128
        self.MG = self.MT // 2
        self.HT = HID // 128
        self.ECH = D // 512
        self.SUBS = self.NC2
        self.WLB = min(8, self.HT // 2)
        assert self.HT % 2 == 0 and self.HT % self.WLB == 0
        assert HEADS * 128 == D and self.NC2 >= 1 and self.MT % 4 == 0


def build_nc(c: Cfg):
    nc = bass.Bass("TRN2", target_bir_lowering=False, dynamic_dma_scratch_size=16384)

    x_d = nc.dram_tensor("x", [c.TOK, c.D], F32, kind="ExternalInput").ap()
    wqT = nc.dram_tensor("wqT", [c.D, c.D], F32, kind="ExternalInput").ap()
    wgT = nc.dram_tensor("wgT", [c.D, c.D], F32, kind="ExternalInput").ap()
    kT_d = nc.dram_tensor("kT", [c.D, c.M], F32, kind="ExternalInput").ap()
    vR_d = nc.dram_tensor("vR", [c.HEADS * c.M, 128], F32, kind="ExternalInput").ap()
    woT = nc.dram_tensor("woT", [c.D, c.D], F32, kind="ExternalInput").ap()
    wg0_d = nc.dram_tensor("wg0", [c.D, c.HID], F32, kind="ExternalInput").ap()
    wg1_d = nc.dram_tensor("wg1", [c.D, c.HID], F32, kind="ExternalInput").ap()
    wl_d = nc.dram_tensor("wl", [c.HID, c.D], F32, kind="ExternalInput").ap()
    out_d = nc.dram_tensor("out", [c.TOK, c.D], F32, kind="ExternalOutput").ap()
    wg08_d = nc.dram_tensor("wg08", [c.HT, 128, c.DT * 128], FP8, kind="Internal").ap()
    wg18_d = nc.dram_tensor("wg18", [c.HT, 128, c.DT * 128], FP8, kind="Internal").ap()
    wl8_d = nc.dram_tensor("wl8", [c.ECH * (c.HT // c.WLB), 128, c.WLB * 512], FP8, kind="Internal").ap()

    act_bytes = max(c.HT * 512, c.HEADS * c.TOKH * 2)
    wa8_bytes = 4 * (c.DT * 128) + 2 * (c.WLB * 512)
    wa_B = 2 * (c.DT * 128) + 2 * (2 * c.M)
    wa_C = 2 * (c.DT * 512)
    wa_D = 4 * (c.DT * 128) + 2 * (c.WLB * 512)
    wa_cols = max(wa_B, wa_C, wa_D)

    ctx = []

    def alloc(name, shape, dtype):
        g = nc.sbuf_tensor(name, shape, dtype)
        t = g.__enter__()
        ctx.append(g)
        return t

    bigT = alloc("big", [128, act_bytes], FP8)
    ag_v = nc.alloc_sbuf_tensor_at("ag", [128, c.HEADS * c.TOKH], BF16,
                                   offset=nc.lookup_mloc(bigT).addr)
    xn_T = alloc("Xn", [128, c.DT, c.TOKH], BF16)
    s_sb = alloc("s_sb", [128, c.TT, c.D], BF16)
    wa = alloc("wa", [128, wa_cols], BF16)
    assert wa8_bytes <= 2 * wa_cols
    wa8 = nc.alloc_sbuf_tensor_at("wa8", [128, wa8_bytes], FP8,
                                  offset=nc.lookup_mloc(wa).addr)
    xst = alloc("xst", [128, 4, 512], F32)
    stage8 = nc.alloc_sbuf_tensor_at("stage8", [128, 2, max(c.DT * 128, c.WLB * 512)], FP8,
                                     offset=nc.lookup_mloc(xst).addr)
    xf = alloc("xf", [128, c.D], F32)
    xn_tok = alloc("xn_tok", [128, c.D], BF16)
    xn8_T = alloc("Xn8", [128, c.DT, c.TOKH], FP8)
    xn8_tok = alloc("xn8_tok", [128, c.D], FP8)
    xn8b_tok = alloc("xn8b", [128, c.D], BF16)
    tr8 = alloc("tr8", [128, c.DT, 128], BF16)
    a_bf = alloc("a_bf", [128, 512], BF16)
    b_bf = alloc("b_bf", [128, 512], BF16)
    act_bf = alloc("act_bf", [128, 512], BF16)
    q_sb = alloc("q_sb", [128, c.TOKH], BF16)
    gate_sb = alloc("gate_sb", [128, c.TOKH], BF16)
    e_sb = alloc("e_sb", [128, 2, 1024], BF16)
    ob = alloc("ob", [128, 4, 512], F32)
    gl_sb = alloc("gl_sb", [128, 512], BF16)
    gf_sb = alloc("gf_sb", [128, 2, 512], BF16)
    stats = alloc("stats", [128, 12], F32)

    psA_g = nc.psum_tensor("psA", [128, 2048], F32)
    psA = psA_g.__enter__(); ctx.append(psA_g)
    psB_g = nc.psum_tensor("psB", [128, 2048], F32)
    psB = psB_g.__enter__(); ctx.append(psB_g)

    off_wq = 0
    off_wg = off_wq + c.DT * 128
    off_kt = off_wg + c.DT * 128
    off_v = off_kt + 2 * c.M
    off_wo = 0
    off_w0 = 0
    off_w1 = off_w0 + 2 * (c.DT * 128)
    off_wl = off_w1 + 2 * (c.DT * 128)

    sem_names = ["XA", "XS", "TX", "OS", "VN", "SSQ", "VRC", "SQT", "QMM", "GMM",
                 "QEV", "GEV", "EMG", "SIL", "OGM", "AGV", "SMM", "SEV",
                 "FFA", "FFB", "GLU", "ACT", "FMM", "GSI", "CMB", "FWL",
                 "WQ", "WG", "KT0", "KT1", "V0", "V1", "WO0", "WO1",
                 "W0A", "W0B", "W1A", "W1B", "WLA", "WLB", "V8", "T8", "XC", "AEV",
                 "SW0A", "SW0B", "SW1A", "SW1B", "SWLA", "SWLB", "PLD0", "PLD1", "PST0", "PST1"]
    sems = {}
    for nm in sem_names:
        g = nc.semaphore(nm)
        sems[nm] = g.__enter__()
        ctx.append(g)

    cnt = defaultdict(int)

    def bump(name, by=1):
        cnt[name] += by
        return cnt[name]

    ops = []

    def op(engine):
        def deco(fn):
            ops.append((engine, fn))
        return deco

    gw = {}
    sx_rec = {}
    prod_store_fns = []

    n_prod = None

    def emit_prod(k, inline_store=False):
        par = k % 2
        pld = "PLD%d" % par
        pst = "PST%d" % par
        pld_t = bump(pld, 16)

        @op("gpsimd")
        def _(e, k=k, par=par, pld=pld, pst=pst):
            if k >= 2:
                e.wait_ge(sems[pst], 16 * (k // 2))
            dst = stage8[:, par, :]
            if k < 2 * c.HT:
                ht = k // 2
                wsrc = wg0_d if k % 2 == 0 else wg1_d
                s_ = wsrc.rearrange("(t p) f -> p t f", p=128)[:, :, ht * 128:(ht + 1) * 128]
                e.dma_start(out=dst[:, 0:c.DT * 128].rearrange("p (t f) -> p t f", f=128),
                            in_=s_).then_inc(sems[pld], 16)
            else:
                j = k - 2 * c.HT
                nb = c.HT // c.WLB
                ec, lb = divmod(j, nb)
                s_ = wl_d[lb * (c.WLB * 128):(lb + 1) * (c.WLB * 128), ec * 512:(ec + 1) * 512] \
                    .rearrange("(g p) f -> p g f", p=128)
                e.dma_start(out=dst[:, 0:c.WLB * 512].rearrange("p (g f) -> p g f", f=512),
                            in_=s_).then_inc(sems[pld], 16)

        bump(pst, 16)

        def _store(e, k=k, par=par, pld=pld, pst=pst, pldt=pld_t):
            e.wait_ge(sems[pld], pldt)
            if k < 2 * c.HT:
                ht = k // 2
                wdst = wg08_d if k % 2 == 0 else wg18_d
                e.dma_start(out=wdst[ht], in_=stage8[:, par, 0:c.DT * 128]).then_inc(sems[pst], 16)
            else:
                j = k - 2 * c.HT
                e.dma_start(out=wl8_d[j], in_=stage8[:, par, 0:c.WLB * 512]).then_inc(sems[pst], 16)
        prod_store_fns.append(_store)

    def wait_l(e, key):
        sname, val = gw[key]
        e.wait_ge(sems[sname], val)

    def emg_head(hg):
        return (hg + 1) * c.NC2 * c.MG

    n_stores = c.HALVES * c.SUBS * c.ECH
    n_combines_half = c.SUBS * c.ECH * 4

    for half in range(c.HALVES):
        HG0 = half * c.HEADS
        if half == 0:
            n_prod = 2 * c.HT + c.ECH * (c.HT // c.WLB)
            skip_heads = min(2, c.HEADS // 4)
            prod_per_head = 4 if c.HEADS >= 8 else 2
            prod_k = 0

        for h in range(c.HEADS):
            if half == 0 and h >= skip_heads:
                for _i in range(prod_per_head):
                    if prod_k < n_prod:
                        emit_prod(prod_k)
                        prod_k += 1
            hg = HG0 + h
            par = hg % 2
            kts = "KT%d" % (hg % 2)
            vs = "V%d" % (hg % 2)
            gw[("wq", hg)] = ("WQ", bump("WQ", 16))
            gw[("wg", hg)] = ("WG", bump("WG", 16))
            gw[("kt", hg)] = (kts, bump(kts, 16))
            gw[("v", hg)] = (vs, bump(vs, 16))

            @op("gpsimd")
            def _(e, h=h, hg=hg, half=half, par=par):
                if hg >= 1:
                    e.wait_ge(sems["QMM"], hg)
                if half > 0 and h < 2:
                    e.wait_ge(sems["FFA"], half * c.SUBS * c.HT)
                    e.wait_ge(sems["FFB"], half * c.SUBS * c.HT)
                    e.wait_ge(sems["FMM"], half * c.SUBS * c.ECH * 4)

                dst = wa[:, off_wq:off_wq + c.DT * 128].rearrange("p (t f) -> p t f", f=128)
                src = wqT.rearrange("(t p) f -> p t f", p=128)[:, :, h * 128:(h + 1) * 128]
                e.dma_start(out=dst, in_=src).then_inc(sems["WQ"], 16)
                if hg >= 1:
                    e.wait_ge(sems["GMM"], hg)
                dst = wa[:, off_wg:off_wg + c.DT * 128].rearrange("p (t f) -> p t f", f=128)
                src = wgT.rearrange("(t p) f -> p t f", p=128)[:, :, h * 128:(h + 1) * 128]
                e.dma_start(out=dst, in_=src).then_inc(sems["WG"], 16)
                if hg >= 2:
                    e.wait_ge(sems["EMG"], emg_head(hg - 2))
                dst = wa[:, off_kt + par * c.M: off_kt + (par + 1) * c.M]
                e.dma_start(out=dst, in_=kT_d[h * 128:(h + 1) * 128, :]).then_inc(sems["KT%d" % (hg % 2)], 16)
                if hg >= 2:
                    e.wait_ge(sems["OGM"], (hg - 1) * c.NC2 * c.MG)
                dst = wa[:, off_v + par * c.M: off_v + (par + 1) * c.M].rearrange("p (mt d) -> p mt d", d=128)
                src = vR_d[h * c.M:(h + 1) * c.M, :].rearrange("(mt p) d -> p mt d", p=128)
                e.dma_start(out=dst, in_=src).then_inc(sems["V%d" % (hg % 2)], 16)

        for ec in range(c.ECH):
            wos = "WO%d" % (ec % 2)
            gw[("wo", half, ec)] = (wos, bump(wos, 16))

            @op("gpsimd")
            def _(e, ec=ec, half=half, HG0=HG0):
                last = HG0 + c.HEADS - 1
                if ec < 2:
                    e.wait_ge(sems["QMM"], last + 1)
                    e.wait_ge(sems["GMM"], last + 1)
                    e.wait_ge(sems["EMG"], emg_head(last))
                    e.wait_ge(sems["OGM"], emg_head(last))

                else:
                    e.wait_ge(sems["SMM"], half * c.ECH * c.TT + (ec - 1) * c.TT)
                p2 = ec % 2
                dst = wa[:, off_wo + p2 * c.DT * 512: off_wo + (p2 + 1) * c.DT * 512] \
                    .rearrange("p (t f) -> p t f", f=512)
                src = woT.rearrange("(t p) f -> p t f", p=128)[:, :, ec * 512:(ec + 1) * 512]
                e.dma_start(out=dst, in_=src).then_inc(sems["WO%d" % (ec % 2)], 16)

        for sb_i in range(c.SUBS):
            subg = half * c.SUBS + sb_i
            for ht in range(c.HT):
                if subg == 0:
                    for _i in range(3):
                        if prod_k < n_prod:
                            emit_prod(prod_k, inline_store=True)
                            prod_k += 1
                s0 = "W0A" if ht % 2 == 0 else "W0B"
                s1 = "W1A" if ht % 2 == 0 else "W1B"
                gw[("w0", subg, ht)] = (s0, bump(s0, 16))
                gw[("w1", subg, ht)] = (s1, bump(s1, 16))

                @op("gpsimd")
                def _(e, ht=ht, subg=subg, half=half, sb_i=sb_i):
                    p2 = ht % 2
                    htg = subg * c.HT + ht
                    if sb_i == 0 and ht < 2:
                        e.wait_ge(sems["SMM"], (half + 1) * c.ECH * c.TT)
                    if subg == 0:
                        e.wait_ge(sems["PST0"], 16 * (ht + 1))
                        e.wait_ge(sems["PST1"], 16 * (ht + 1))
                    if htg >= 2:
                        e.wait_ge(sems["FFA"], htg - 1)
                    dst0 = wa8[:, off_w0 + p2 * (c.DT * 128): off_w0 + (p2 + 1) * (c.DT * 128)]
                    e.dma_start(out=dst0, in_=wg08_d[ht]).then_inc(
                        sems["W0A" if ht % 2 == 0 else "W0B"], 16)
                    if htg >= 2:
                        e.wait_ge(sems["FFB"], htg - 1)
                    dst1 = wa8[:, off_w1 + p2 * (c.DT * 128): off_w1 + (p2 + 1) * (c.DT * 128)]
                    e.dma_start(out=dst1, in_=wg18_d[ht]).then_inc(
                        sems["W1A" if ht % 2 == 0 else "W1B"], 16)

            nb = c.HT // c.WLB
            for ec in range(c.ECH):
                for lb in range(nb):
                    if subg == 0:
                        while prod_k < n_prod and prod_k < 2 * c.HT + (ec * nb + lb + 2):
                            emit_prod(prod_k, inline_store=True)
                            prod_k += 1
                    sl = "WLA" if lb % 2 == 0 else "WLB"
                    gw[("wl", subg, ec, lb)] = (sl, bump(sl, 16))

                    @op("gpsimd")
                    def _(e, ec=ec, lb=lb, subg=subg, nb=nb, half=half, sb_i=sb_i):
                        p2 = lb % 2
                        gwl = (subg * c.ECH + ec) * nb + lb
                        j = ec * nb + lb
                        if sb_i == 0 and ec == 0 and lb < 2:
                            e.wait_ge(sems["SMM"], (half + 1) * c.ECH * c.TT)
                        if subg == 0:
                            kk = 2 * c.HT + j
                            e.wait_ge(sems["PST%d" % (kk % 2)], 16 * (kk // 2 + 1))
                        if gwl >= 2:
                            prev = gwl - 2
                            p_pos = prev % nb
                            p_sweep = prev // nb
                            if p_pos == nb - 1:
                                e.wait_ge(sems["FMM"], (p_sweep + 1) * 4)
                            else:
                                e.wait_ge(sems["FWL"], p_sweep * (nb - 1) + p_pos + 1)
                        dst = wa8[:, off_wl + p2 * (c.WLB * 512): off_wl + (p2 + 1) * (c.WLB * 512)]
                        e.dma_start(out=dst, in_=wl8_d[j]).then_inc(
                            sems["WLA" if lb % 2 == 0 else "WLB"], 16)

        def emit_phase_A(half):
            for tt in range(c.TT):
                g = half * c.TT + tt
                tok0 = half * c.TOKH + tt * 128
                sx_t = bump("XA", 16)

                @op("sync")
                def _(e, g=g, tok0=tok0):
                    if g >= 1:
                        e.wait_ge(sems["VN"], g)
                    e.dma_start(out=xf[:], in_=x_d[tok0:tok0 + 128, :]).then_inc(sems["XA"], 16)

                ssq_t = bump("SSQ", 1)

                @op("scalar")
                def _(e, g=g, sx=sx_t):
                    e.wait_ge(sems["XA"], sx)
                    e.wait_ge(sems["TX"], 16 * g)
                    e.activation(xn_tok[:], xf[:], AF.Square,
                                 accum_out=stats[:, 2:3]).then_inc(sems["SSQ"], 1)

                vrc_t = bump("VRC", 1)

                @op("vector")
                def _(e, g=g, sx=sx_t, ssq=ssq_t):
                    e.wait_ge(sems["XA"], sx)
                    e.reduce_sum(stats[:, 0:1], xf[:], axis=AXX)
                    e.drain()
                    e.tensor_scalar_mul(stats[:, 1:2], stats[:, 0:1], 1.0 / c.D)
                    e.wait_ge(sems["SSQ"], ssq)
                    e.tensor_scalar_mul(stats[:, 3:4], stats[:, 2:3], 1.0 / c.D)
                    e.drain()
                    e.tensor_tensor(stats[:, 4:5], stats[:, 1:2], stats[:, 1:2], ALU.mult)
                    e.drain()
                    e.tensor_sub(stats[:, 5:6], stats[:, 3:4], stats[:, 4:5])
                    e.drain()
                    e.tensor_scalar_add(stats[:, 5:6], stats[:, 5:6], 1e-5)
                    e.drain()
                    e.reciprocal(stats[:, 6:7], stats[:, 5:6]).then_inc(sems["VRC"], 1)

                sqt_t = bump("SQT", 1)

                @op("scalar")
                def _(e, vrc=vrc_t):
                    e.wait_ge(sems["VRC"], vrc)
                    e.activation(stats[:, 7:8], stats[:, 6:7], AF.Sqrt).then_inc(sems["SQT"], 1)

                vn_t = bump("VN", 1)
                v8_t = bump("V8", 1)
                xc_prev = cnt["XC"]
                xc_t = bump("XC", 1)

                @op("vector")
                def _(e, sqt=sqt_t, g=g, tt=tt, half=half, t8=(cnt["T8"] + 16), xcp=xc_prev):
                    e.wait_ge(sems["SQT"], sqt)
                    e.tensor_tensor(stats[:, 8:9], stats[:, 7:8], stats[:, 7:8], ALU.mult)
                    e.drain()
                    e.tensor_tensor(stats[:, 9:10], stats[:, 8:9], stats[:, 5:6], ALU.mult)
                    e.drain()
                    e.tensor_scalar(stats[:, 9:10], stats[:, 9:10], -0.5, 1.5, ALU.mult, ALU.add)
                    e.drain()
                    e.tensor_tensor(stats[:, 10:11], stats[:, 7:8], stats[:, 9:10], ALU.mult)
                    e.drain()
                    e.tensor_scalar(xn_tok[:], xf[:], stats[:, 1:2], stats[:, 10:11],
                                    ALU.subtract, ALU.mult)
                    e.tensor_scalar(xn8_tok[:], xf[:], stats[:, 1:2], stats[:, 10:11],
                                    ALU.subtract, ALU.mult).then_inc(sems["VN"], 1)
                    e.drain()
                    e.tensor_copy(xn8b_tok[:], xn8_tok[:]).then_inc(sems["V8"], 1)
                    e.wait_ge(sems["T8"], t8)
                    if half > 0 and tt == 0:
                        e.wait_ge(sems["FFB"], half * c.SUBS * c.HT)
                    e.tensor_copy(xn8_T[:, :, tt * 128:(tt + 1) * 128], tr8[:]).then_inc(sems["XC"], 1)

                bump("TX", 16)
                bump("T8", 16)

                @op("sync")
                def _(e, g=g, tt=tt, half=half, vn=vn_t, v8=v8_t, xcp=xc_prev):
                    e.wait_ge(sems["VN"], vn)
                    if half > 0 and tt == 0:
                        e.wait_ge(sems["FFB"], half * c.SUBS * c.HT)
                    e.dma_start_transpose(xn_T[:, :, tt * 128:(tt + 1) * 128],
                                          xn_tok[:]).then_inc(sems["TX"], 16)
                    e.wait_ge(sems["V8"], v8)
                    e.wait_ge(sems["XC"], xcp)
                    e.dma_start_transpose(tr8[:], xn8b_tok[:]).then_inc(sems["T8"], 16)

    
        emit_phase_A(half)
        if half == 0:
            for fn_ in prod_store_fns:
                op("sync")(fn_)
            prod_store_fns.clear()

        for h in range(c.HEADS):
            hg = HG0 + h
            par = hg % 2
            qmm_t = bump("QMM", 1)
            assert qmm_t == hg + 1

            @op("tensor")
            def _(e, h=h, hg=hg, half=half):
                wait_l(e, ("wq", hg))
                if h == 0:
                    e.wait_ge(sems["TX"], (half + 1) * c.TT * 16)
                    if half > 0:
                        e.wait_ge(sems["GLU"], half * c.SUBS * c.HT)
                        e.wait_ge(sems["ACT"], half * c.SUBS * c.HT)
                if hg >= 1:
                    e.wait_ge(sems["GEV"], hg)
                wqv = wa[:, off_wq:off_wq + c.DT * 128]
                last = None
                for et in range(c.DT):
                    for nc2 in range(c.NC2):
                        last = e.matmul(psA[:, nc2 * 512:(nc2 + 1) * 512],
                                        wqv[:, et * 128:(et + 1) * 128],
                                        xn_T[:, et, nc2 * 512:(nc2 + 1) * 512],
                                        start=(et == 0), stop=(et == c.DT - 1))
                last.then_inc(sems["QMM"], 1)

            qev_t = bump("QEV", 1)

            @op("vector")
            def _(e, qmm=qmm_t):
                e.wait_ge(sems["QMM"], qmm)
                e.tensor_copy(q_sb[:, 0:c.NC2 * 512], psA[:, 0:c.NC2 * 512]).then_inc(sems["QEV"], 1)

            gmm_t = bump("GMM", 1)

            @op("tensor")
            def _(e, hg=hg, qev=qev_t):
                wait_l(e, ("wg", hg))
                e.wait_ge(sems["QEV"], qev)
                wgv = wa[:, off_wg:off_wg + c.DT * 128]
                last = None
                for et in range(c.DT):
                    for nc2 in range(c.NC2):
                        last = e.matmul(psA[:, nc2 * 512:(nc2 + 1) * 512],
                                        wgv[:, et * 128:(et + 1) * 128],
                                        xn_T[:, et, nc2 * 512:(nc2 + 1) * 512],
                                        start=(et == 0), stop=(et == c.DT - 1))
                last.then_inc(sems["GMM"], 1)

            gev_t = bump("GEV", 1)

            @op("scalar")
            def _(e, hg=hg, gmm=gmm_t):
                e.wait_ge(sems["GMM"], gmm)
                if hg >= 1:
                    e.wait_ge(sems["AGV"], hg * c.NC2)
                e.activation(gate_sb[:, 0:c.NC2 * 512], psA[:, 0:c.NC2 * 512], AF.Silu).then_inc(sems["GEV"], 1)

            for nc2 in range(c.NC2):
                gbase = (hg * c.NC2 + nc2) * c.MG
                o_off = 1024 + (nc2 % 2) * 512

                for grp in range(c.MG):
                    assert bump("EMG", 1) == gbase + grp + 1
                sil_base = cnt["SIL"]
                for grp in range(c.MG):
                    assert bump("SIL", 1) == gbase + grp + 1
                for grp in range(c.MG):
                    assert bump("OGM", 1) == gbase + grp + 1

                @op("tensor")
                def _(e, hg=hg, h=h, nc2=nc2, par=par, half=half, gbase=gbase, o_off=o_off):
                    if nc2 == 0:
                        wait_l(e, ("kt", hg))
                        wait_l(e, ("v", hg))

                    def emit_E(grp):
                        gidx = gbase + grp
                        epar = gidx % 2
                        if gidx >= 2:
                            e.wait_ge(sems["SIL"], gidx - 1)
                        if half > 0 and h == 0 and nc2 == 0 and grp == 0:
                            e.wait_ge(sems["GSI"], half * c.SUBS * c.ECH * 4)
                        ktv = wa[:, off_kt + par * c.M: off_kt + (par + 1) * c.M]
                        qv = q_sb[:, nc2 * 512:(nc2 + 1) * 512]
                        last = None
                        for i in range(2):
                            mt = grp * 2 + i
                            last = e.matmul(psB[:, epar * 1024 + i * 512: epar * 1024 + (i + 1) * 512],
                                            ktv[:, mt * 128:(mt + 1) * 128], qv,
                                            start=True, stop=True)
                        last.then_inc(sems["EMG"], 1)

                    def emit_O(grp):
                        gidx = gbase + grp
                        epar = gidx % 2
                        e.wait_ge(sems["SIL"], gidx + 1)
                        if grp == 0 and hg >= 1:
                            e.wait_ge(sems["AGV"], (hg - 1) * c.NC2 + nc2 + 1)
                        vv = wa[:, off_v + par * c.M: off_v + (par + 1) * c.M]
                        last = None
                        for i in range(2):
                            mt = grp * 2 + i
                            last = e.matmul(psA[:, o_off:o_off + 512],
                                            vv[:, mt * 128:(mt + 1) * 128],
                                            e_sb[:, epar, i * 512:(i + 1) * 512],
                                            start=(grp == 0 and i == 0),
                                            stop=(grp == c.MG - 1 and i == 1))
                        last.then_inc(sems["OGM"], 1)

                    emit_E(0)
                    for grp in range(1, c.MG):
                        emit_E(grp)
                        emit_O(grp - 1)
                    emit_O(c.MG - 1)

                @op("scalar")
                def _(e, gbase=gbase):
                    for grp in range(c.MG):
                        gidx = gbase + grp
                        epar = gidx % 2
                        e.wait_ge(sems["EMG"], gidx + 1)
                        if gidx >= 2:
                            e.wait_ge(sems["OGM"], gidx - 1)
                        e.activation(e_sb[:, epar, :], psB[:, epar * 1024:(epar + 1) * 1024],
                                     AF.Silu, scale=SCALE).then_inc(sems["SIL"], 1)

                agv_t = bump("AGV", 1)
                assert agv_t == hg * c.NC2 + nc2 + 1

                @op("vector")
                def _(e, hg=hg, h=h, nc2=nc2, par=par, gbase=gbase, o_off=o_off, half=half):
                    e.wait_ge(sems["OGM"], gbase + c.MG)
                    if half > 0 and h == 0 and nc2 == 0:
                        e.wait_ge(sems["FMM"], half * c.SUBS * c.ECH * 4)
                    e.scalar_tensor_tensor(ag_v[:, h * c.TOKH + nc2 * 512: h * c.TOKH + (nc2 + 1) * 512],
                                           psA[:, o_off:o_off + 512], SCALE,
                                           gate_sb[:, nc2 * 512:(nc2 + 1) * 512],
                                           ALU.mult, ALU.mult).then_inc(sems["AGV"], 1)

        for ec in range(c.ECH):
            for nt in range(c.TT):
                sidx = half * c.ECH * c.TT + ec * c.TT + nt
                par_ec = ec % 2
                slot = (sidx % 2) * 512
                smm_t = bump("SMM", 1)
                assert smm_t == sidx + 1

                @op("tensor")
                def _(e, ec=ec, nt=nt, sidx=sidx, par_ec=par_ec, slot=slot, half=half):
                    wait_l(e, ("wo", half, ec))
                    if ec == 0 and nt == 0:
                        e.wait_ge(sems["AGV"], (half + 1) * c.HEADS * c.NC2)
                        e.wait_ge(sems["GEV"], (half + 1) * c.HEADS)
                    if sidx >= 2:
                        e.wait_ge(sems["SEV"], sidx - 1)
                    last = None
                    for vt in range(c.DT):
                        lhs = ag_v[:, vt * c.TOKH + nt * 128: vt * c.TOKH + (nt + 1) * 128]
                        rhs = wa[:, off_wo + par_ec * c.DT * 512 + vt * 512:
                                 off_wo + par_ec * c.DT * 512 + (vt + 1) * 512]
                        last = e.matmul(psA[:, slot:slot + 512], lhs, rhs,
                                        start=(vt == 0), stop=(vt == c.DT - 1))
                    last.then_inc(sems["SMM"], 1)

                sev_t = bump("SEV", 1)

                @op("vector")
                def _(e, ec=ec, nt=nt, smm=smm_t, slot=slot, half=half):
                    e.wait_ge(sems["SMM"], smm)
                    if ec == 0 and nt == 0:
                        e.wait_ge(sems["CMB"], half * n_combines_half)
                    e.tensor_copy(s_sb[:, nt, ec * 512:(ec + 1) * 512],
                                  psA[:, slot:slot + 512]).then_inc(sems["SEV"], 1)

        for sb_i in range(c.SUBS):
            subg = half * c.SUBS + sb_i
            if half == 0 and sb_i == 1:
                while prod_k < n_prod:
                    emit_prod(prod_k, inline_store=True)
                    prod_k += 1
            for ht in range(c.HT):
                htg = subg * c.HT + ht
                a_off = (ht % 2) * 1024
                b_off = a_off + 512
                ffa_t = bump("FFA", 1)
                ffb_t = bump("FFB", 1)
                assert ffa_t == htg + 1

                @op("tensor")
                def _(e, ht=ht, htg=htg, a_off=a_off, b_off=b_off,
                      sb_i=sb_i, subg=subg, half=half):
                    p2 = ht % 2
                    wait_l(e, ("w0", subg, ht))
                    if htg >= 2:
                        e.wait_ge(sems["AEV"], htg - 1)
                    if ht == 0:
                        e.wait_ge(sems["SEV"], (half + 1) * c.ECH * c.TT)
                        e.wait_ge(sems["AGV"], (half + 1) * c.HEADS * c.NC2)
                        e.wait_ge(sems["XC"], (half + 1) * c.TT)
                    w0v = wa8[:, off_w0 + p2 * (c.DT * 128): off_w0 + (p2 + 1) * (c.DT * 128)] \
                        .rearrange("p (t f) -> p t f", f=128)
                    last = None
                    nd = c.DT // 2
                    for j in range(nd):
                        last = e.matmul(psA[:, a_off:a_off + 512], w0v[:, 2 * j:2 * j + 2, :],
                                        xn8_T[:, 2 * j:2 * j + 2, sb_i * 512:(sb_i + 1) * 512],
                                        start=(j == 0), stop=(j == nd - 1), perf_mode=DRMODE)
                    last.then_inc(sems["FFA"], 1)
                    wait_l(e, ("w1", subg, ht))
                    w1v = wa8[:, off_w1 + p2 * (c.DT * 128): off_w1 + (p2 + 1) * (c.DT * 128)] \
                        .rearrange("p (t f) -> p t f", f=128)
                    for j in range(nd):
                        last = e.matmul(psA[:, b_off:b_off + 512], w1v[:, 2 * j:2 * j + 2, :],
                                        xn8_T[:, 2 * j:2 * j + 2, sb_i * 512:(sb_i + 1) * 512],
                                        start=(j == 0), stop=(j == nd - 1), perf_mode=DRMODE)
                    last.then_inc(sems["FFB"], 1)

                aev_t = bump("AEV", 1)
                glu_t = bump("GLU", 1)
                act_t = bump("ACT", 1)
                assert act_t == htg + 1

                @op("vector")
                def _(e, ht=ht, htg=htg, a_off=a_off, b_off=b_off, subg=subg, sb_i=sb_i,
                      half=half, glu=glu_t, ffa=ffa_t, ffb=ffb_t):
                    e.wait_ge(sems["FFA"], ffa)
                    if htg >= 1:
                        e.wait_ge(sems["GLU"], htg)
                    e.tensor_copy(a_bf[:], psA[:, a_off:a_off + 512])
                    e.wait_ge(sems["FFB"], ffb)
                    e.drain()
                    e.tensor_copy(b_bf[:], psA[:, b_off:b_off + 512]).then_inc(sems["AEV"], 1)
                    e.wait_ge(sems["GLU"], glu)
                    e.tensor_tensor(act_bf[:], gl_sb[:], b_bf[:], ALU.mult)
                    e.drain()
                    if ht == 0:
                        if subg >= 1:
                            e.wait_ge(sems["FMM"], subg * c.ECH * 4)
                        if sb_i == 0:
                            e.wait_ge(sems["SMM"], (half + 1) * c.ECH * c.TT)
                    e.tensor_copy(bigT[:, ht * 512:(ht + 1) * 512],
                                  act_bf[:]).then_inc(sems["ACT"], 1)

                @op("scalar")
                def _(e, htg=htg, aev=aev_t):
                    e.wait_ge(sems["AEV"], aev)
                    if htg >= 1:
                        e.wait_ge(sems["ACT"], htg)
                    e.activation(gl_sb[:], a_bf[:],
                                 AF.Gelu_apprx_tanh).then_inc(sems["GLU"], 1)


            nb = c.HT // c.WLB
            nhp = c.HT // 2
            for ec in range(c.ECH):
                for htp in range(nhp):
                    ht0 = 2 * htp
                    lb2 = ht0 // c.WLB
                    bp = lb2 % 2
                    if htp == nhp - 1:
                        for nt in range(4):
                            assert bump("FMM", 1) == subg * c.ECH * 4 + ec * 4 + nt + 1
                    if (ht0 + 1) % c.WLB == c.WLB - 1 and lb2 != nb - 1:
                        bump("FWL", 1)

                    @op("tensor")
                    def _(e, ec=ec, htp=htp, ht0=ht0, lb2=lb2, bp=bp, subg=subg, nb=nb, nhp=nhp):
                        if htp == 0:
                            for nt in range(4):
                                fidx = subg * c.ECH * 4 + ec * 4 + nt
                                if fidx >= 4:
                                    e.wait_ge(sems["GSI"], fidx - 3)
                        wait_l(e, ("wl", subg, ec, lb2))
                        e.wait_ge(sems["ACT"], subg * c.HT + ht0 + 2)
                        wlv = wa8[:, off_wl + bp * (c.WLB * 512): off_wl + (bp + 1) * (c.WLB * 512)] \
                            .rearrange("p (g f) -> p g f", f=512)
                        actv = bigT[:, ht0 * 512:(ht0 + 2) * 512].rearrange("p (k x) -> p k x", k=2)
                        last = None
                        for nt in range(4):
                            last = e.matmul(psB[:, nt * 512:(nt + 1) * 512],
                                            actv[:, :, nt * 128:(nt + 1) * 128],
                                            wlv[:, ht0 % c.WLB: ht0 % c.WLB + 2, :],
                                            start=(htp == 0), stop=(htp == nhp - 1), perf_mode=DRMODE)
                            if htp == nhp - 1:
                                last.then_inc(sems["FMM"], 1)
                        if (ht0 + 1) % c.WLB == c.WLB - 1 and lb2 != nb - 1:
                            last.then_inc(sems["FWL"], 1)

                for nt in range(4):
                    fidx = subg * c.ECH * 4 + ec * 4 + nt
                    s_id = subg * c.ECH + ec
                    gsi_t = bump("GSI", 1)
                    assert gsi_t == fidx + 1

                    @op("scalar")
                    def _(e, nt=nt, fidx=fidx):
                        e.wait_ge(sems["FMM"], fidx + 1)
                        if fidx >= 2:
                            e.wait_ge(sems["CMB"], fidx - 1)
                        e.activation(gf_sb[:, nt % 2, :], psB[:, nt * 512:(nt + 1) * 512],
                                     AF.Silu).then_inc(sems["GSI"], 1)

                    cmb_t = bump("CMB", 1)
                    assert cmb_t == fidx + 1

                    @op("vector")
                    def _(e, nt=nt, gsi=gsi_t, s_id=s_id, ec=ec, sb_i=sb_i):
                        e.wait_ge(sems["GSI"], gsi)
                        if nt == 0:
                            e.wait_ge(sems["XS"], sx_rec[("xst", s_id)])
                            e.wait_ge(sems["OS"], 16 * s_id)
                            e.drain()
                        e.scalar_tensor_tensor(ob[:, nt, :],
                                               s_sb[:, sb_i * 4 + nt, ec * 512:(ec + 1) * 512], 1.0,
                                               gf_sb[:, nt % 2, :], ALU.mult, ALU.mult)
                        e.drain()
                        e.tensor_tensor(ob[:, nt, :], ob[:, nt, :], xst[:, nt, :],
                                        ALU.add).then_inc(sems["CMB"], 1)

            tok0 = half * c.TOKH + sb_i * 512
            for ec in range(c.ECH):
                s_id = subg * c.ECH + ec
                sx_rec[("xst", s_id)] = bump("XS", 16)

                @op("sync")
                def _(e, ec=ec, s_id=s_id, tok0=tok0):
                    e.wait_ge(sems["CMB"], s_id * 4)
                    if s_id == 0:
                        np_tot = 2 * c.HT + c.ECH * (c.HT // c.WLB)
                        e.wait_ge(sems["PST0"], 16 * ((np_tot + 1) // 2))
                        e.wait_ge(sems["PST1"], 16 * (np_tot // 2))
                    src = x_d[tok0:tok0 + 512, ec * 512:(ec + 1) * 512].rearrange("(t p) f -> p t f", p=128)
                    e.dma_start(out=xst[:], in_=src).then_inc(sems["XS"], 16)


                bump("OS", 16)

                @op("sync")
                def _(e, ec=ec, s_id=s_id, tok0=tok0):
                    e.wait_ge(sems["CMB"], (s_id + 1) * 4)
                    dst = out_d[tok0:tok0 + 512, ec * 512:(ec + 1) * 512].rearrange("(t p) f -> p t f", p=128)
                    e.dma_start(out=dst, in_=ob[:]).then_inc(sems["OS"], 16)

    @op("sync")
    def _(e):
        e.wait_ge(sems["OS"], 16 * n_stores)

    assert cnt["QMM"] == c.HALVES * c.HEADS
    assert cnt["EMG"] == c.HALVES * c.HEADS * c.NC2 * c.MG
    assert cnt["FMM"] == c.HALVES * c.SUBS * c.ECH * 4
    assert cnt["OS"] == 16 * n_stores

    import os
    lim = int(os.environ.get("OPS_LIMIT", "0")) or len(ops)
    ops = ops[:lim]
    with nc.Block() as block:
        @block.sync
        def _(eng):
            for name, fn in ops:
                if name == "sync":
                    fn(eng)

        @block.gpsimd
        def _(eng):
            for name, fn in ops:
                if name == "gpsimd":
                    fn(eng)

        @block.tensor
        def _(eng):
            for name, fn in ops:
                if name == "tensor":
                    fn(eng)

        @block.scalar
        def _(eng):
            for name, fn in ops:
                if name == "scalar":
                    fn(eng)

        @block.vector
        def _(eng):
            for name, fn in ops:
                if name == "vector":
                    fn(eng)

    for g in reversed(ctx):
        g.__exit__(None, None, None)
    return nc


def prep_inputs(L0_attn_x, L0_attn_Wq, L0_attn_k_weight, L0_attn_v_weight,
                L0_attn_Wg, L0_attn_Wout, L0_ffn_w_gating, L0_ffn_w_linear, n_cores=8):
    f = np.float32
    xs = np.asarray(L0_attn_x, f)
    x = np.ascontiguousarray(xs.reshape(-1, xs.shape[-1]))
    wq = np.asarray(L0_attn_Wq, f)
    kw = np.asarray(L0_attn_k_weight, f)
    vw = np.asarray(L0_attn_v_weight, f)
    wg = np.asarray(L0_attn_Wg, f)
    wo = np.asarray(L0_attn_Wout, f)
    wgat = np.asarray(L0_ffn_w_gating, f)
    wlin = np.asarray(L0_ffn_w_linear, f)
    D = wq.shape[1]
    H = D // 128
    M = kw.shape[0]
    shared = {
        "wqT": np.ascontiguousarray(wq.T),
        "wgT": np.ascontiguousarray(wg.T),
        "kT": np.ascontiguousarray(kw.T),
        "vR": np.ascontiguousarray(vw.reshape(M, H, 128).transpose(1, 0, 2).reshape(H * M, 128)),
        "woT": np.ascontiguousarray(wo.T),
        "wg0": np.ascontiguousarray(wgat[0]),
        "wg1": np.ascontiguousarray(wgat[1]),
        "wl": np.ascontiguousarray(wlin),
    }
    tok = x.shape[0] // n_cores
    return [dict(shared, x=np.ascontiguousarray(x[i * tok:(i + 1) * tok])) for i in range(n_cores)]


def kernel(L0_attn_x, L0_attn_Wq, L0_attn_k_weight, L0_attn_v_weight,
           L0_attn_Wg, L0_attn_Wout, L0_ffn_w_gating, L0_ffn_w_linear):
    from concourse.bass_utils import run_bass_kernel_spmd
    in_maps = prep_inputs(L0_attn_x, L0_attn_Wq, L0_attn_k_weight, L0_attn_v_weight,
                          L0_attn_Wg, L0_attn_Wout, L0_ffn_w_gating, L0_ffn_w_linear)
    nc = build_nc(Cfg())
    res = run_bass_kernel_spmd(nc, in_maps, core_ids=list(range(8)))
    out = np.concatenate([res.results[i]["out"] for i in range(8)], axis=0)
    return out.reshape(np.asarray(L0_attn_x).shape)




# version 12
# speedup vs baseline: 1.0832x; 1.0832x over previous
from collections import defaultdict

import numpy as np

import concourse.bass as bass
import concourse.mybir as mybir

F32 = mybir.dt.float32
BF16 = mybir.dt.bfloat16
FP8 = mybir.dt.float8e4
AF = mybir.ActivationFunctionType
ALU = mybir.AluOpType
AXX = mybir.AxisListType.X
DRMODE = mybir.MatmulPerfMode.DoubleRow

SCALE = float(np.float32(128.0) ** np.float32(0.5))


class Cfg:
    def __init__(self, D=2048, HID=8192, M=4096, HEADS=16, TOK=2048, HALVES=2):
        self.D, self.HID, self.M, self.HEADS, self.TOK, self.HALVES = D, HID, M, HEADS, TOK, HALVES
        self.TOKH = TOK // HALVES
        self.TT = self.TOKH // 128
        self.NC2 = self.TOKH // 512
        self.DT = D // 128
        self.MT = M // 128
        self.MG = self.MT // 2
        self.HT = HID // 128
        self.ECH = D // 512
        self.SUBS = self.NC2
        self.WLB = min(8, self.HT // 2)
        assert self.HT % 2 == 0 and self.HT % self.WLB == 0
        assert HEADS * 128 == D and self.NC2 >= 1 and self.MT % 4 == 0


def build_nc(c: Cfg):
    nc = bass.Bass("TRN2", target_bir_lowering=False, dynamic_dma_scratch_size=16384)

    x_d = nc.dram_tensor("x", [c.TOK, c.D], F32, kind="ExternalInput").ap()
    wqT = nc.dram_tensor("wqT", [c.D, c.D], F32, kind="ExternalInput").ap()
    wgT = nc.dram_tensor("wgT", [c.D, c.D], F32, kind="ExternalInput").ap()
    kT_d = nc.dram_tensor("kT", [c.D, c.M], F32, kind="ExternalInput").ap()
    vR_d = nc.dram_tensor("vR", [c.HEADS * c.M, 128], F32, kind="ExternalInput").ap()
    woT = nc.dram_tensor("woT", [c.D, c.D], F32, kind="ExternalInput").ap()
    wg0_d = nc.dram_tensor("wg0", [c.D, c.HID], F32, kind="ExternalInput").ap()
    wg1_d = nc.dram_tensor("wg1", [c.D, c.HID], F32, kind="ExternalInput").ap()
    wl_d = nc.dram_tensor("wl", [c.HID, c.D], F32, kind="ExternalInput").ap()
    out_d = nc.dram_tensor("out", [c.TOK, c.D], F32, kind="ExternalOutput").ap()
    wg08_d = nc.dram_tensor("wg08", [c.HT, 128, c.DT * 128], FP8, kind="Internal").ap()
    wg18_d = nc.dram_tensor("wg18", [c.HT, 128, c.DT * 128], FP8, kind="Internal").ap()
    wl8_d = nc.dram_tensor("wl8", [c.ECH * (c.HT // c.WLB), 128, c.WLB * 512], FP8, kind="Internal").ap()

    act_bytes = max(c.HT * 512, c.HEADS * c.TOKH * 2)
    wa8_bytes = 4 * (c.DT * 128) + 2 * (c.WLB * 512)
    wa_B = 2 * (c.DT * 128) + 2 * (2 * c.M)
    wa_C = 2 * (c.DT * 512)
    wa_D = 4 * (c.DT * 128) + 2 * (c.WLB * 512)
    wa_cols = max(wa_B, wa_C, wa_D)

    ctx = []

    def alloc(name, shape, dtype):
        g = nc.sbuf_tensor(name, shape, dtype)
        t = g.__enter__()
        ctx.append(g)
        return t

    bigT = alloc("big", [128, act_bytes], FP8)
    ag_v = nc.alloc_sbuf_tensor_at("ag", [128, c.HEADS * c.TOKH], BF16,
                                   offset=nc.lookup_mloc(bigT).addr)
    xn_T = alloc("Xn", [128, c.DT, c.TOKH], BF16)
    s_sb = alloc("s_sb", [128, c.TT, c.D], BF16)
    wa = alloc("wa", [128, wa_cols], BF16)
    assert wa8_bytes <= 2 * wa_cols
    wa8 = nc.alloc_sbuf_tensor_at("wa8", [128, wa8_bytes], FP8,
                                  offset=nc.lookup_mloc(wa).addr)
    xst = alloc("xst", [128, 4, 512], F32)
    stage8 = nc.alloc_sbuf_tensor_at("stage8", [128, 2, max(c.DT * 128, c.WLB * 512)], FP8,
                                     offset=nc.lookup_mloc(xst).addr)
    xf = alloc("xf", [128, c.D], F32)
    xn_tok = alloc("xn_tok", [128, c.D], BF16)
    xn8_T = alloc("Xn8", [128, c.DT, c.TOKH], FP8)
    xn8_tok = alloc("xn8_tok", [128, c.D], FP8)
    xn8b_tok = alloc("xn8b", [128, c.D], BF16)
    tr8 = alloc("tr8", [128, c.DT, 128], BF16)
    a_bf = alloc("a_bf", [128, 512], BF16)
    b_bf = alloc("b_bf", [128, 512], BF16)
    act_bf = alloc("act_bf", [128, 512], BF16)
    q_sb = alloc("q_sb", [128, c.TOKH], BF16)
    gate_sb = alloc("gate_sb", [128, c.TOKH], BF16)
    e_sb = alloc("e_sb", [128, 2, 1024], BF16)
    ob = alloc("ob", [128, 4, 512], F32)
    gl_sb = alloc("gl_sb", [128, 512], BF16)
    gf_sb = alloc("gf_sb", [128, 2, 512], BF16)
    stats = alloc("stats", [128, 12], F32)

    psA_g = nc.psum_tensor("psA", [128, 2048], F32)
    psA = psA_g.__enter__(); ctx.append(psA_g)
    psB_g = nc.psum_tensor("psB", [128, 2048], F32)
    psB = psB_g.__enter__(); ctx.append(psB_g)

    off_wq = 0
    off_wg = off_wq + c.DT * 128
    off_kt = off_wg + c.DT * 128
    off_v = off_kt + 2 * c.M
    off_wo = 0
    off_w0 = 0
    off_w1 = off_w0 + 2 * (c.DT * 128)
    off_wl = off_w1 + 2 * (c.DT * 128)

    sem_names = ["XA", "XS", "TX", "OS", "VN", "SSQ", "VRC", "SQT", "QMM", "GMM",
                 "QEV", "GEV", "EMG", "SIL", "OGM", "AGV", "SMM", "SEV",
                 "FFA", "FFB", "GLU", "ACT", "FMM", "GSI", "CMB", "FWL",
                 "WQ", "WG", "KT0", "KT1", "V0", "V1", "WO0", "WO1",
                 "W0A", "W0B", "W1A", "W1B", "WLA", "WLB", "V8", "T8", "XC", "AEV",
                 "SW0A", "SW0B", "SW1A", "SW1B", "SWLA", "SWLB", "PLD0", "PLD1", "PST0", "PST1"]
    sems = {}
    for nm in sem_names:
        g = nc.semaphore(nm)
        sems[nm] = g.__enter__()
        ctx.append(g)

    cnt = defaultdict(int)

    def bump(name, by=1):
        cnt[name] += by
        return cnt[name]

    ops = []

    def op(engine):
        def deco(fn):
            ops.append((engine, fn))
        return deco

    gw = {}
    sx_rec = {}
    prod_store_fns = []

    n_prod = None

    def emit_prod(k, inline_store=False):
        par = k % 2
        pld = "PLD%d" % par
        pst = "PST%d" % par
        pld_t = bump(pld, 16)

        @op("gpsimd")
        def _(e, k=k, par=par, pld=pld, pst=pst):
            if k >= 2:
                e.wait_ge(sems[pst], 16 * (k // 2))
            dst = stage8[:, par, :]
            if k < 2 * c.HT:
                ht = k // 2
                wsrc = wg0_d if k % 2 == 0 else wg1_d
                s_ = wsrc.rearrange("(t p) f -> p t f", p=128)[:, :, ht * 128:(ht + 1) * 128]
                e.dma_start(out=dst[:, 0:c.DT * 128].rearrange("p (t f) -> p t f", f=128),
                            in_=s_).then_inc(sems[pld], 16)
            else:
                j = k - 2 * c.HT
                nb = c.HT // c.WLB
                ec, lb = divmod(j, nb)
                s_ = wl_d[lb * (c.WLB * 128):(lb + 1) * (c.WLB * 128), ec * 512:(ec + 1) * 512] \
                    .rearrange("(g p) f -> p g f", p=128)
                e.dma_start(out=dst[:, 0:c.WLB * 512].rearrange("p (g f) -> p g f", f=512),
                            in_=s_).then_inc(sems[pld], 16)

        bump(pst, 16)

        def _store(e, k=k, par=par, pld=pld, pst=pst, pldt=pld_t):
            e.wait_ge(sems[pld], pldt)
            if k < 2 * c.HT:
                ht = k // 2
                wdst = wg08_d if k % 2 == 0 else wg18_d
                e.dma_start(out=wdst[ht], in_=stage8[:, par, 0:c.DT * 128]).then_inc(sems[pst], 16)
            else:
                j = k - 2 * c.HT
                e.dma_start(out=wl8_d[j], in_=stage8[:, par, 0:c.WLB * 512]).then_inc(sems[pst], 16)
        prod_store_fns.append(_store)

    def wait_l(e, key):
        sname, val = gw[key]
        e.wait_ge(sems[sname], val)

    def emg_head(hg):
        return (hg + 1) * c.NC2 * c.MG

    n_stores = c.HALVES * c.SUBS * c.ECH
    n_combines_half = c.SUBS * c.ECH * 4

    for half in range(c.HALVES):
        HG0 = half * c.HEADS
        if half == 0:
            n_prod = 2 * c.HT + c.ECH * (c.HT // c.WLB)
            skip_heads = min(2, c.HEADS // 4)
            prod_per_head = 8 if c.HEADS >= 8 else 2
            prod_k = 0

        for h in range(c.HEADS):
            if half == 0 and h >= skip_heads:
                for _i in range(prod_per_head):
                    if prod_k < n_prod:
                        emit_prod(prod_k)
                        prod_k += 1
            hg = HG0 + h
            par = hg % 2
            kts = "KT%d" % (hg % 2)
            vs = "V%d" % (hg % 2)
            gw[("wq", hg)] = ("WQ", bump("WQ", 16))
            gw[("wg", hg)] = ("WG", bump("WG", 16))
            gw[("kt", hg)] = (kts, bump(kts, 16))
            gw[("v", hg)] = (vs, bump(vs, 16))

            @op("gpsimd")
            def _(e, h=h, hg=hg, half=half, par=par):
                if hg >= 1:
                    e.wait_ge(sems["QMM"], hg)
                if half > 0 and h < 2:
                    e.wait_ge(sems["FFA"], half * c.SUBS * c.HT)
                    e.wait_ge(sems["FFB"], half * c.SUBS * c.HT)
                    e.wait_ge(sems["FMM"], half * c.SUBS * c.ECH * 4)

                dst = wa[:, off_wq:off_wq + c.DT * 128].rearrange("p (t f) -> p t f", f=128)
                src = wqT.rearrange("(t p) f -> p t f", p=128)[:, :, h * 128:(h + 1) * 128]
                e.dma_start(out=dst, in_=src).then_inc(sems["WQ"], 16)
                if hg >= 1:
                    e.wait_ge(sems["GMM"], hg)
                dst = wa[:, off_wg:off_wg + c.DT * 128].rearrange("p (t f) -> p t f", f=128)
                src = wgT.rearrange("(t p) f -> p t f", p=128)[:, :, h * 128:(h + 1) * 128]
                e.dma_start(out=dst, in_=src).then_inc(sems["WG"], 16)
                if hg >= 2:
                    e.wait_ge(sems["EMG"], emg_head(hg - 2))
                dst = wa[:, off_kt + par * c.M: off_kt + (par + 1) * c.M]
                e.dma_start(out=dst, in_=kT_d[h * 128:(h + 1) * 128, :]).then_inc(sems["KT%d" % (hg % 2)], 16)
                if hg >= 2:
                    e.wait_ge(sems["OGM"], (hg - 1) * c.NC2 * c.MG)
                dst = wa[:, off_v + par * c.M: off_v + (par + 1) * c.M].rearrange("p (mt d) -> p mt d", d=128)
                src = vR_d[h * c.M:(h + 1) * c.M, :].rearrange("(mt p) d -> p mt d", p=128)
                e.dma_start(out=dst, in_=src).then_inc(sems["V%d" % (hg % 2)], 16)

        if half == 0:
            while prod_k < n_prod:
                emit_prod(prod_k)
                prod_k += 1

        for ec in range(c.ECH):
            wos = "WO%d" % (ec % 2)
            gw[("wo", half, ec)] = (wos, bump(wos, 16))

            @op("gpsimd")
            def _(e, ec=ec, half=half, HG0=HG0):
                last = HG0 + c.HEADS - 1
                if ec < 2:
                    e.wait_ge(sems["QMM"], last + 1)
                    e.wait_ge(sems["GMM"], last + 1)
                    e.wait_ge(sems["EMG"], emg_head(last))
                    e.wait_ge(sems["OGM"], emg_head(last))

                else:
                    e.wait_ge(sems["SMM"], half * c.ECH * c.TT + (ec - 1) * c.TT)
                p2 = ec % 2
                dst = wa[:, off_wo + p2 * c.DT * 512: off_wo + (p2 + 1) * c.DT * 512] \
                    .rearrange("p (t f) -> p t f", f=512)
                src = woT.rearrange("(t p) f -> p t f", p=128)[:, :, ec * 512:(ec + 1) * 512]
                e.dma_start(out=dst, in_=src).then_inc(sems["WO%d" % (ec % 2)], 16)

        for sb_i in range(c.SUBS):
            subg = half * c.SUBS + sb_i
            for ht in range(c.HT):
                s0 = "W0A" if ht % 2 == 0 else "W0B"
                s1 = "W1A" if ht % 2 == 0 else "W1B"
                gw[("w0", subg, ht)] = (s0, bump(s0, 16))
                gw[("w1", subg, ht)] = (s1, bump(s1, 16))

                @op("gpsimd")
                def _(e, ht=ht, subg=subg, half=half, sb_i=sb_i):
                    p2 = ht % 2
                    htg = subg * c.HT + ht
                    if sb_i == 0 and ht < 2:
                        e.wait_ge(sems["SMM"], (half + 1) * c.ECH * c.TT)
                    if subg == 0:
                        e.wait_ge(sems["PST0"], 16 * (ht + 1))
                        e.wait_ge(sems["PST1"], 16 * (ht + 1))
                    if htg >= 2:
                        e.wait_ge(sems["FFA"], htg - 1)
                    dst0 = wa8[:, off_w0 + p2 * (c.DT * 128): off_w0 + (p2 + 1) * (c.DT * 128)]
                    e.dma_start(out=dst0, in_=wg08_d[ht]).then_inc(
                        sems["W0A" if ht % 2 == 0 else "W0B"], 16)
                    if htg >= 2:
                        e.wait_ge(sems["FFB"], htg - 1)
                    dst1 = wa8[:, off_w1 + p2 * (c.DT * 128): off_w1 + (p2 + 1) * (c.DT * 128)]
                    e.dma_start(out=dst1, in_=wg18_d[ht]).then_inc(
                        sems["W1A" if ht % 2 == 0 else "W1B"], 16)

            nb = c.HT // c.WLB
            for ec in range(c.ECH):
                for lb in range(nb):
                    sl = "WLA" if lb % 2 == 0 else "WLB"
                    gw[("wl", subg, ec, lb)] = (sl, bump(sl, 16))

                    @op("gpsimd")
                    def _(e, ec=ec, lb=lb, subg=subg, nb=nb, half=half, sb_i=sb_i):
                        p2 = lb % 2
                        gwl = (subg * c.ECH + ec) * nb + lb
                        j = ec * nb + lb
                        if sb_i == 0 and ec == 0 and lb < 2:
                            e.wait_ge(sems["SMM"], (half + 1) * c.ECH * c.TT)
                        if subg == 0:
                            kk = 2 * c.HT + j
                            e.wait_ge(sems["PST%d" % (kk % 2)], 16 * (kk // 2 + 1))
                        if gwl >= 2:
                            prev = gwl - 2
                            p_pos = prev % nb
                            p_sweep = prev // nb
                            if p_pos == nb - 1:
                                e.wait_ge(sems["FMM"], (p_sweep + 1) * 4)
                            else:
                                e.wait_ge(sems["FWL"], p_sweep * (nb - 1) + p_pos + 1)
                        dst = wa8[:, off_wl + p2 * (c.WLB * 512): off_wl + (p2 + 1) * (c.WLB * 512)]
                        e.dma_start(out=dst, in_=wl8_d[j]).then_inc(
                            sems["WLA" if lb % 2 == 0 else "WLB"], 16)

        def emit_phase_A(half):
            for tt in range(c.TT):
                g = half * c.TT + tt
                tok0 = half * c.TOKH + tt * 128
                sx_t = bump("XA", 16)

                @op("sync")
                def _(e, g=g, tok0=tok0):
                    if g >= 1:
                        e.wait_ge(sems["VN"], g)
                    e.dma_start(out=xf[:], in_=x_d[tok0:tok0 + 128, :]).then_inc(sems["XA"], 16)

                ssq_t = bump("SSQ", 1)

                @op("scalar")
                def _(e, g=g, sx=sx_t):
                    e.wait_ge(sems["XA"], sx)
                    e.wait_ge(sems["TX"], 16 * g)
                    e.activation(xn_tok[:], xf[:], AF.Square,
                                 accum_out=stats[:, 2:3]).then_inc(sems["SSQ"], 1)

                vrc_t = bump("VRC", 1)

                @op("vector")
                def _(e, g=g, sx=sx_t, ssq=ssq_t):
                    e.wait_ge(sems["XA"], sx)
                    e.reduce_sum(stats[:, 0:1], xf[:], axis=AXX)
                    e.drain()
                    e.tensor_scalar_mul(stats[:, 1:2], stats[:, 0:1], 1.0 / c.D)
                    e.wait_ge(sems["SSQ"], ssq)
                    e.tensor_scalar_mul(stats[:, 3:4], stats[:, 2:3], 1.0 / c.D)
                    e.drain()
                    e.tensor_tensor(stats[:, 4:5], stats[:, 1:2], stats[:, 1:2], ALU.mult)
                    e.drain()
                    e.tensor_sub(stats[:, 5:6], stats[:, 3:4], stats[:, 4:5])
                    e.drain()
                    e.tensor_scalar_add(stats[:, 5:6], stats[:, 5:6], 1e-5)
                    e.drain()
                    e.reciprocal(stats[:, 6:7], stats[:, 5:6]).then_inc(sems["VRC"], 1)

                sqt_t = bump("SQT", 1)

                @op("scalar")
                def _(e, vrc=vrc_t):
                    e.wait_ge(sems["VRC"], vrc)
                    e.activation(stats[:, 7:8], stats[:, 6:7], AF.Sqrt).then_inc(sems["SQT"], 1)

                vn_t = bump("VN", 1)
                v8_t = bump("V8", 1)
                xc_prev = cnt["XC"]
                xc_t = bump("XC", 1)

                @op("vector")
                def _(e, sqt=sqt_t, g=g, tt=tt, half=half, t8=(cnt["T8"] + 16), xcp=xc_prev):
                    e.wait_ge(sems["SQT"], sqt)
                    e.tensor_tensor(stats[:, 8:9], stats[:, 7:8], stats[:, 7:8], ALU.mult)
                    e.drain()
                    e.tensor_tensor(stats[:, 9:10], stats[:, 8:9], stats[:, 5:6], ALU.mult)
                    e.drain()
                    e.tensor_scalar(stats[:, 9:10], stats[:, 9:10], -0.5, 1.5, ALU.mult, ALU.add)
                    e.drain()
                    e.tensor_tensor(stats[:, 10:11], stats[:, 7:8], stats[:, 9:10], ALU.mult)
                    e.drain()
                    e.tensor_scalar(xn_tok[:], xf[:], stats[:, 1:2], stats[:, 10:11],
                                    ALU.subtract, ALU.mult)
                    e.tensor_scalar(xn8_tok[:], xf[:], stats[:, 1:2], stats[:, 10:11],
                                    ALU.subtract, ALU.mult).then_inc(sems["VN"], 1)
                    e.drain()
                    e.tensor_copy(xn8b_tok[:], xn8_tok[:]).then_inc(sems["V8"], 1)
                    e.wait_ge(sems["T8"], t8)
                    if half > 0 and tt == 0:
                        e.wait_ge(sems["FFB"], half * c.SUBS * c.HT)
                    e.tensor_copy(xn8_T[:, :, tt * 128:(tt + 1) * 128], tr8[:]).then_inc(sems["XC"], 1)

                bump("TX", 16)
                bump("T8", 16)

                @op("sync")
                def _(e, g=g, tt=tt, half=half, vn=vn_t, v8=v8_t, xcp=xc_prev):
                    e.wait_ge(sems["VN"], vn)
                    if half > 0 and tt == 0:
                        e.wait_ge(sems["FFB"], half * c.SUBS * c.HT)
                    e.dma_start_transpose(xn_T[:, :, tt * 128:(tt + 1) * 128],
                                          xn_tok[:]).then_inc(sems["TX"], 16)
                    e.wait_ge(sems["V8"], v8)
                    e.wait_ge(sems["XC"], xcp)
                    e.dma_start_transpose(tr8[:], xn8b_tok[:]).then_inc(sems["T8"], 16)

    
        emit_phase_A(half)
        if half == 0:
            for fn_ in prod_store_fns:
                op("sync")(fn_)
            prod_store_fns.clear()

        for h in range(c.HEADS):
            hg = HG0 + h
            par = hg % 2
            qmm_t = bump("QMM", 1)
            assert qmm_t == hg + 1

            @op("tensor")
            def _(e, h=h, hg=hg, half=half):
                wait_l(e, ("wq", hg))
                if h == 0:
                    e.wait_ge(sems["TX"], (half + 1) * c.TT * 16)
                    if half > 0:
                        e.wait_ge(sems["GLU"], half * c.SUBS * c.HT)
                        e.wait_ge(sems["ACT"], half * c.SUBS * c.HT)
                if hg >= 1:
                    e.wait_ge(sems["GEV"], hg)
                wqv = wa[:, off_wq:off_wq + c.DT * 128]
                last = None
                for et in range(c.DT):
                    for nc2 in range(c.NC2):
                        last = e.matmul(psA[:, nc2 * 512:(nc2 + 1) * 512],
                                        wqv[:, et * 128:(et + 1) * 128],
                                        xn_T[:, et, nc2 * 512:(nc2 + 1) * 512],
                                        start=(et == 0), stop=(et == c.DT - 1))
                last.then_inc(sems["QMM"], 1)

            qev_t = bump("QEV", 1)

            @op("vector")
            def _(e, qmm=qmm_t):
                e.wait_ge(sems["QMM"], qmm)
                e.tensor_copy(q_sb[:, 0:c.NC2 * 512], psA[:, 0:c.NC2 * 512]).then_inc(sems["QEV"], 1)

            gmm_t = bump("GMM", 1)

            @op("tensor")
            def _(e, hg=hg, qev=qev_t):
                wait_l(e, ("wg", hg))
                e.wait_ge(sems["QEV"], qev)
                wgv = wa[:, off_wg:off_wg + c.DT * 128]
                last = None
                for et in range(c.DT):
                    for nc2 in range(c.NC2):
                        last = e.matmul(psA[:, nc2 * 512:(nc2 + 1) * 512],
                                        wgv[:, et * 128:(et + 1) * 128],
                                        xn_T[:, et, nc2 * 512:(nc2 + 1) * 512],
                                        start=(et == 0), stop=(et == c.DT - 1))
                last.then_inc(sems["GMM"], 1)

            gev_t = bump("GEV", 1)

            @op("scalar")
            def _(e, hg=hg, gmm=gmm_t):
                e.wait_ge(sems["GMM"], gmm)
                if hg >= 1:
                    e.wait_ge(sems["AGV"], hg * c.NC2)
                e.activation(gate_sb[:, 0:c.NC2 * 512], psA[:, 0:c.NC2 * 512], AF.Silu).then_inc(sems["GEV"], 1)

            for nc2 in range(c.NC2):
                gbase = (hg * c.NC2 + nc2) * c.MG
                o_off = 1024 + (nc2 % 2) * 512

                for grp in range(c.MG):
                    assert bump("EMG", 1) == gbase + grp + 1
                sil_base = cnt["SIL"]
                for grp in range(c.MG):
                    assert bump("SIL", 1) == gbase + grp + 1
                for grp in range(c.MG):
                    assert bump("OGM", 1) == gbase + grp + 1

                @op("tensor")
                def _(e, hg=hg, h=h, nc2=nc2, par=par, half=half, gbase=gbase, o_off=o_off):
                    if nc2 == 0:
                        wait_l(e, ("kt", hg))
                        wait_l(e, ("v", hg))

                    def emit_E(grp):
                        gidx = gbase + grp
                        epar = gidx % 2
                        if gidx >= 2:
                            e.wait_ge(sems["SIL"], gidx - 1)
                        if half > 0 and h == 0 and nc2 == 0 and grp == 0:
                            e.wait_ge(sems["GSI"], half * c.SUBS * c.ECH * 4)
                        ktv = wa[:, off_kt + par * c.M: off_kt + (par + 1) * c.M]
                        qv = q_sb[:, nc2 * 512:(nc2 + 1) * 512]
                        last = None
                        for i in range(2):
                            mt = grp * 2 + i
                            last = e.matmul(psB[:, epar * 1024 + i * 512: epar * 1024 + (i + 1) * 512],
                                            ktv[:, mt * 128:(mt + 1) * 128], qv,
                                            start=True, stop=True)
                        last.then_inc(sems["EMG"], 1)

                    def emit_O(grp):
                        gidx = gbase + grp
                        epar = gidx % 2
                        e.wait_ge(sems["SIL"], gidx + 1)
                        if grp == 0 and hg >= 1:
                            e.wait_ge(sems["AGV"], (hg - 1) * c.NC2 + nc2 + 1)
                        vv = wa[:, off_v + par * c.M: off_v + (par + 1) * c.M]
                        last = None
                        for i in range(2):
                            mt = grp * 2 + i
                            last = e.matmul(psA[:, o_off:o_off + 512],
                                            vv[:, mt * 128:(mt + 1) * 128],
                                            e_sb[:, epar, i * 512:(i + 1) * 512],
                                            start=(grp == 0 and i == 0),
                                            stop=(grp == c.MG - 1 and i == 1))
                        last.then_inc(sems["OGM"], 1)

                    emit_E(0)
                    for grp in range(1, c.MG):
                        emit_E(grp)
                        emit_O(grp - 1)
                    emit_O(c.MG - 1)

                @op("scalar")
                def _(e, gbase=gbase):
                    for grp in range(c.MG):
                        gidx = gbase + grp
                        epar = gidx % 2
                        e.wait_ge(sems["EMG"], gidx + 1)
                        if gidx >= 2:
                            e.wait_ge(sems["OGM"], gidx - 1)
                        e.activation(e_sb[:, epar, :], psB[:, epar * 1024:(epar + 1) * 1024],
                                     AF.Silu, scale=SCALE).then_inc(sems["SIL"], 1)

                agv_t = bump("AGV", 1)
                assert agv_t == hg * c.NC2 + nc2 + 1

                @op("vector")
                def _(e, hg=hg, h=h, nc2=nc2, par=par, gbase=gbase, o_off=o_off, half=half):
                    e.wait_ge(sems["OGM"], gbase + c.MG)
                    if half > 0 and h == 0 and nc2 == 0:
                        e.wait_ge(sems["FMM"], half * c.SUBS * c.ECH * 4)
                    e.scalar_tensor_tensor(ag_v[:, h * c.TOKH + nc2 * 512: h * c.TOKH + (nc2 + 1) * 512],
                                           psA[:, o_off:o_off + 512], SCALE,
                                           gate_sb[:, nc2 * 512:(nc2 + 1) * 512],
                                           ALU.mult, ALU.mult).then_inc(sems["AGV"], 1)

        for ec in range(c.ECH):
            for nt in range(c.TT):
                sidx = half * c.ECH * c.TT + ec * c.TT + nt
                par_ec = ec % 2
                slot = (sidx % 2) * 512
                smm_t = bump("SMM", 1)
                assert smm_t == sidx + 1

                @op("tensor")
                def _(e, ec=ec, nt=nt, sidx=sidx, par_ec=par_ec, slot=slot, half=half):
                    wait_l(e, ("wo", half, ec))
                    if ec == 0 and nt == 0:
                        e.wait_ge(sems["AGV"], (half + 1) * c.HEADS * c.NC2)
                        e.wait_ge(sems["GEV"], (half + 1) * c.HEADS)
                    if sidx >= 2:
                        e.wait_ge(sems["SEV"], sidx - 1)
                    last = None
                    for vt in range(c.DT):
                        lhs = ag_v[:, vt * c.TOKH + nt * 128: vt * c.TOKH + (nt + 1) * 128]
                        rhs = wa[:, off_wo + par_ec * c.DT * 512 + vt * 512:
                                 off_wo + par_ec * c.DT * 512 + (vt + 1) * 512]
                        last = e.matmul(psA[:, slot:slot + 512], lhs, rhs,
                                        start=(vt == 0), stop=(vt == c.DT - 1))
                    last.then_inc(sems["SMM"], 1)

                sev_t = bump("SEV", 1)

                @op("vector")
                def _(e, ec=ec, nt=nt, smm=smm_t, slot=slot, half=half):
                    e.wait_ge(sems["SMM"], smm)
                    if ec == 0 and nt == 0:
                        e.wait_ge(sems["CMB"], half * n_combines_half)
                    e.tensor_copy(s_sb[:, nt, ec * 512:(ec + 1) * 512],
                                  psA[:, slot:slot + 512]).then_inc(sems["SEV"], 1)

        for sb_i in range(c.SUBS):
            subg = half * c.SUBS + sb_i
            for ht in range(c.HT):
                htg = subg * c.HT + ht
                a_off = (ht % 2) * 1024
                b_off = a_off + 512
                ffa_t = bump("FFA", 1)
                ffb_t = bump("FFB", 1)
                assert ffa_t == htg + 1

                @op("tensor")
                def _(e, ht=ht, htg=htg, a_off=a_off, b_off=b_off,
                      sb_i=sb_i, subg=subg, half=half):
                    p2 = ht % 2
                    wait_l(e, ("w0", subg, ht))
                    if htg >= 2:
                        e.wait_ge(sems["AEV"], htg - 1)
                    if ht == 0:
                        e.wait_ge(sems["SEV"], (half + 1) * c.ECH * c.TT)
                        e.wait_ge(sems["AGV"], (half + 1) * c.HEADS * c.NC2)
                        e.wait_ge(sems["XC"], (half + 1) * c.TT)
                    w0v = wa8[:, off_w0 + p2 * (c.DT * 128): off_w0 + (p2 + 1) * (c.DT * 128)] \
                        .rearrange("p (t f) -> p t f", f=128)
                    last = None
                    nd = c.DT // 2
                    for j in range(nd):
                        last = e.matmul(psA[:, a_off:a_off + 512], w0v[:, 2 * j:2 * j + 2, :],
                                        xn8_T[:, 2 * j:2 * j + 2, sb_i * 512:(sb_i + 1) * 512],
                                        start=(j == 0), stop=(j == nd - 1), perf_mode=DRMODE)
                    last.then_inc(sems["FFA"], 1)
                    wait_l(e, ("w1", subg, ht))
                    w1v = wa8[:, off_w1 + p2 * (c.DT * 128): off_w1 + (p2 + 1) * (c.DT * 128)] \
                        .rearrange("p (t f) -> p t f", f=128)
                    for j in range(nd):
                        last = e.matmul(psA[:, b_off:b_off + 512], w1v[:, 2 * j:2 * j + 2, :],
                                        xn8_T[:, 2 * j:2 * j + 2, sb_i * 512:(sb_i + 1) * 512],
                                        start=(j == 0), stop=(j == nd - 1), perf_mode=DRMODE)
                    last.then_inc(sems["FFB"], 1)

                aev_t = bump("AEV", 1)
                glu_t = bump("GLU", 1)
                act_t = bump("ACT", 1)
                assert act_t == htg + 1

                @op("vector")
                def _(e, ht=ht, htg=htg, a_off=a_off, b_off=b_off, subg=subg, sb_i=sb_i,
                      half=half, glu=glu_t, ffa=ffa_t, ffb=ffb_t):
                    e.wait_ge(sems["FFA"], ffa)
                    if htg >= 1:
                        e.wait_ge(sems["GLU"], htg)
                    e.tensor_copy(a_bf[:], psA[:, a_off:a_off + 512])
                    e.wait_ge(sems["FFB"], ffb)
                    e.drain()
                    e.tensor_copy(b_bf[:], psA[:, b_off:b_off + 512]).then_inc(sems["AEV"], 1)
                    e.wait_ge(sems["GLU"], glu)
                    e.tensor_tensor(act_bf[:], gl_sb[:], b_bf[:], ALU.mult)
                    e.drain()
                    if ht == 0:
                        if subg >= 1:
                            e.wait_ge(sems["FMM"], subg * c.ECH * 4)
                        if sb_i == 0:
                            e.wait_ge(sems["SMM"], (half + 1) * c.ECH * c.TT)
                    e.tensor_copy(bigT[:, ht * 512:(ht + 1) * 512],
                                  act_bf[:]).then_inc(sems["ACT"], 1)

                @op("scalar")
                def _(e, htg=htg, aev=aev_t):
                    e.wait_ge(sems["AEV"], aev)
                    if htg >= 1:
                        e.wait_ge(sems["ACT"], htg)
                    e.activation(gl_sb[:], a_bf[:],
                                 AF.Gelu_apprx_tanh).then_inc(sems["GLU"], 1)


            nb = c.HT // c.WLB
            nhp = c.HT // 2
            for ec in range(c.ECH):
                for htp in range(nhp):
                    ht0 = 2 * htp
                    lb2 = ht0 // c.WLB
                    bp = lb2 % 2
                    if htp == nhp - 1:
                        for nt in range(4):
                            assert bump("FMM", 1) == subg * c.ECH * 4 + ec * 4 + nt + 1
                    if (ht0 + 1) % c.WLB == c.WLB - 1 and lb2 != nb - 1:
                        bump("FWL", 1)

                    @op("tensor")
                    def _(e, ec=ec, htp=htp, ht0=ht0, lb2=lb2, bp=bp, subg=subg, nb=nb, nhp=nhp):
                        if htp == 0:
                            for nt in range(4):
                                fidx = subg * c.ECH * 4 + ec * 4 + nt
                                if fidx >= 4:
                                    e.wait_ge(sems["GSI"], fidx - 3)
                        wait_l(e, ("wl", subg, ec, lb2))
                        e.wait_ge(sems["ACT"], subg * c.HT + ht0 + 2)
                        wlv = wa8[:, off_wl + bp * (c.WLB * 512): off_wl + (bp + 1) * (c.WLB * 512)] \
                            .rearrange("p (g f) -> p g f", f=512)
                        actv = bigT[:, ht0 * 512:(ht0 + 2) * 512].rearrange("p (k x) -> p k x", k=2)
                        last = None
                        for nt in range(4):
                            last = e.matmul(psB[:, nt * 512:(nt + 1) * 512],
                                            actv[:, :, nt * 128:(nt + 1) * 128],
                                            wlv[:, ht0 % c.WLB: ht0 % c.WLB + 2, :],
                                            start=(htp == 0), stop=(htp == nhp - 1), perf_mode=DRMODE)
                            if htp == nhp - 1:
                                last.then_inc(sems["FMM"], 1)
                        if (ht0 + 1) % c.WLB == c.WLB - 1 and lb2 != nb - 1:
                            last.then_inc(sems["FWL"], 1)

                for nt in range(4):
                    fidx = subg * c.ECH * 4 + ec * 4 + nt
                    s_id = subg * c.ECH + ec
                    gsi_t = bump("GSI", 1)
                    assert gsi_t == fidx + 1

                    @op("scalar")
                    def _(e, nt=nt, fidx=fidx):
                        e.wait_ge(sems["FMM"], fidx + 1)
                        if fidx >= 2:
                            e.wait_ge(sems["CMB"], fidx - 1)
                        e.activation(gf_sb[:, nt % 2, :], psB[:, nt * 512:(nt + 1) * 512],
                                     AF.Silu).then_inc(sems["GSI"], 1)

                    cmb_t = bump("CMB", 1)
                    assert cmb_t == fidx + 1

                    @op("vector")
                    def _(e, nt=nt, gsi=gsi_t, s_id=s_id, ec=ec, sb_i=sb_i):
                        e.wait_ge(sems["GSI"], gsi)
                        if nt == 0:
                            e.wait_ge(sems["XS"], sx_rec[("xst", s_id)])
                            e.wait_ge(sems["OS"], 16 * s_id)
                            e.drain()
                        e.scalar_tensor_tensor(ob[:, nt, :],
                                               s_sb[:, sb_i * 4 + nt, ec * 512:(ec + 1) * 512], 1.0,
                                               gf_sb[:, nt % 2, :], ALU.mult, ALU.mult)
                        e.drain()
                        e.tensor_tensor(ob[:, nt, :], ob[:, nt, :], xst[:, nt, :],
                                        ALU.add).then_inc(sems["CMB"], 1)

            tok0 = half * c.TOKH + sb_i * 512
            for ec in range(c.ECH):
                s_id = subg * c.ECH + ec
                sx_rec[("xst", s_id)] = bump("XS", 16)

                @op("sync")
                def _(e, ec=ec, s_id=s_id, tok0=tok0):
                    e.wait_ge(sems["CMB"], s_id * 4)
                    if s_id == 0:
                        np_tot = 2 * c.HT + c.ECH * (c.HT // c.WLB)
                        e.wait_ge(sems["PST0"], 16 * ((np_tot + 1) // 2))
                        e.wait_ge(sems["PST1"], 16 * (np_tot // 2))
                    src = x_d[tok0:tok0 + 512, ec * 512:(ec + 1) * 512].rearrange("(t p) f -> p t f", p=128)
                    e.dma_start(out=xst[:], in_=src).then_inc(sems["XS"], 16)


                bump("OS", 16)

                @op("sync")
                def _(e, ec=ec, s_id=s_id, tok0=tok0):
                    e.wait_ge(sems["CMB"], (s_id + 1) * 4)
                    dst = out_d[tok0:tok0 + 512, ec * 512:(ec + 1) * 512].rearrange("(t p) f -> p t f", p=128)
                    e.dma_start(out=dst, in_=ob[:]).then_inc(sems["OS"], 16)

    @op("sync")
    def _(e):
        e.wait_ge(sems["OS"], 16 * n_stores)

    assert cnt["QMM"] == c.HALVES * c.HEADS
    assert cnt["EMG"] == c.HALVES * c.HEADS * c.NC2 * c.MG
    assert cnt["FMM"] == c.HALVES * c.SUBS * c.ECH * 4
    assert cnt["OS"] == 16 * n_stores

    import os
    lim = int(os.environ.get("OPS_LIMIT", "0")) or len(ops)
    ops = ops[:lim]
    with nc.Block() as block:
        @block.sync
        def _(eng):
            for name, fn in ops:
                if name == "sync":
                    fn(eng)

        @block.gpsimd
        def _(eng):
            for name, fn in ops:
                if name == "gpsimd":
                    fn(eng)

        @block.tensor
        def _(eng):
            for name, fn in ops:
                if name == "tensor":
                    fn(eng)

        @block.scalar
        def _(eng):
            for name, fn in ops:
                if name == "scalar":
                    fn(eng)

        @block.vector
        def _(eng):
            for name, fn in ops:
                if name == "vector":
                    fn(eng)

    for g in reversed(ctx):
        g.__exit__(None, None, None)
    return nc


def prep_inputs(L0_attn_x, L0_attn_Wq, L0_attn_k_weight, L0_attn_v_weight,
                L0_attn_Wg, L0_attn_Wout, L0_ffn_w_gating, L0_ffn_w_linear, n_cores=8):
    f = np.float32
    xs = np.asarray(L0_attn_x, f)
    x = np.ascontiguousarray(xs.reshape(-1, xs.shape[-1]))
    wq = np.asarray(L0_attn_Wq, f)
    kw = np.asarray(L0_attn_k_weight, f)
    vw = np.asarray(L0_attn_v_weight, f)
    wg = np.asarray(L0_attn_Wg, f)
    wo = np.asarray(L0_attn_Wout, f)
    wgat = np.asarray(L0_ffn_w_gating, f)
    wlin = np.asarray(L0_ffn_w_linear, f)
    D = wq.shape[1]
    H = D // 128
    M = kw.shape[0]
    shared = {
        "wqT": np.ascontiguousarray(wq.T),
        "wgT": np.ascontiguousarray(wg.T),
        "kT": np.ascontiguousarray(kw.T),
        "vR": np.ascontiguousarray(vw.reshape(M, H, 128).transpose(1, 0, 2).reshape(H * M, 128)),
        "woT": np.ascontiguousarray(wo.T),
        "wg0": np.ascontiguousarray(wgat[0]),
        "wg1": np.ascontiguousarray(wgat[1]),
        "wl": np.ascontiguousarray(wlin),
    }
    tok = x.shape[0] // n_cores
    return [dict(shared, x=np.ascontiguousarray(x[i * tok:(i + 1) * tok])) for i in range(n_cores)]


def kernel(L0_attn_x, L0_attn_Wq, L0_attn_k_weight, L0_attn_v_weight,
           L0_attn_Wg, L0_attn_Wout, L0_ffn_w_gating, L0_ffn_w_linear):
    from concourse.bass_utils import run_bass_kernel_spmd
    in_maps = prep_inputs(L0_attn_x, L0_attn_Wq, L0_attn_k_weight, L0_attn_v_weight,
                          L0_attn_Wg, L0_attn_Wout, L0_ffn_w_gating, L0_ffn_w_linear)
    nc = build_nc(Cfg())
    res = run_bass_kernel_spmd(nc, in_maps, core_ids=list(range(8)))
    out = np.concatenate([res.results[i]["out"] for i in range(8)], axis=0)
    return out.reshape(np.asarray(L0_attn_x).shape)


